# version 02
# baseline (speedup 1.0000x reference)
import numpy as np
import ml_dtypes
import concourse.bass as bass
import concourse.mybir as mybir
from concourse.bass_utils import run_bass_kernel_spmd
from contextlib import ExitStack

F32 = mybir.dt.float32
BF16 = mybir.dt.bfloat16
FP8 = mybir.dt.float8e4
AF = mybir.ActivationFunctionType
ALU = mybir.AluOpType
PM = mybir.MatmulPerfMode
E = mybir.EngineType
AX = mybir.AxisListType
ts = bass.ts

S = 4096
H = 2048
HID = 8192
NH = H // 128
NS = S // 128
NHID = HID // 128

BFnp = ml_dtypes.bfloat16
SCALE = float(np.float32(1.0) / np.float32(np.sqrt(np.float32(2048.0))))
CG1 = float(np.sqrt(2.0 / np.pi).astype(BFnp))
CG2 = float(np.asarray(0.044715, dtype=BFnp))
CQ_MARGIN = 1.02 * SCALE

ENG = [E.SP, E.PE, E.DVE, E.Activation]


def build_nc(upto=99):
    nc = bass.Bass("TRN2", target_bir_lowering=False, debug=False)

    x_h = nc.dram_tensor("x", [S, H], F32, kind="ExternalInput").ap()
    wqkvT_h = nc.dram_tensor("wqkvT", [H, 3 * H], F32, kind="ExternalInput").ap()
    woT_h = nc.dram_tensor("woT", [H, H], F32, kind="ExternalInput").ap()
    wg0_h = nc.dram_tensor("wg0", [H, HID], F32, kind="ExternalInput").ap()
    wg1_h = nc.dram_tensor("wg1", [H, HID], F32, kind="ExternalInput").ap()
    wl_h = nc.dram_tensor("wl", [HID, H], F32, kind="ExternalInput").ap()
    bpm_h = nc.dram_tensor("bpm", [48, 128], F32, kind="ExternalInput").ap()
    bobc_h = nc.dram_tensor("bobc", [128, H], F32, kind="ExternalInput").ap()
    id32_h = nc.dram_tensor("id32", [128, 128], F32, kind="ExternalInput").ap()
    idbf_h = nc.dram_tensor("idbf", [128, 128], BF16, kind="ExternalInput").ap()
    out_h = nc.dram_tensor("out", [S, H], F32, kind="ExternalOutput").ap()

    xnbf_h = nc.dram_tensor("xnbf", [H, S], BF16, kind="Internal").ap()
    xnq8_h = nc.dram_tensor("xnq8", [H, S], FP8, kind="Internal").ap()
    q_h = nc.dram_tensor("q_s", [H, S], BF16, kind="Internal").ap()
    k_h = nc.dram_tensor("k_s", [H, S], BF16, kind="Internal").ap()
    vt_h = nc.dram_tensor("vt_s", [S, H], BF16, kind="Internal").ap()
    cq_h = nc.dram_tensor("cq_s", [S], F32, kind="Internal").ap()
    inv_h = nc.dram_tensor("inv_s", [S], F32, kind="Internal").ap()
    sat_h = nc.dram_tensor("sat_s", [S, H], F32, kind="Internal").ap()
    act_h = nc.dram_tensor("act_s", [HID, S], FP8, kind="Internal").ap()

    id32 = nc.alloc_sbuf_tensor("id32s", [128, 128], F32)
    idbf = nc.alloc_sbuf_tensor("idbfs", [128, 128], BF16)
    onesb = nc.alloc_sbuf_tensor("onesb", [128, 1], BF16)
    ldc = nc.alloc_semaphore("ldc")

    nc.sync.dma_start(out=id32[:], in_=id32_h[:, :]).then_inc(ldc, 16)
    nc.sync.dma_start(out=idbf[:], in_=idbf_h[:, :]).then_inc(ldc, 16)
    nc.vector.memset(onesb[:], 1.0)
    nc.tensor.wait_ge(ldc, 32)
    nc.all_engine_barrier()

    fin_sem = nc.alloc_semaphore("fin_sem")

    def _finish_early():
        nc.sync.dma_start(out=out_h[0:128, 0:128], in_=id32[:]).then_inc(fin_sem, 16)
        nc.sync.wait_ge(fin_sem, 16)
        return nc

    with ExitStack() as p0:
        xbuf = [p0.enter_context(nc.sbuf_tensor(f"xbuf{j}", [128, H], F32)) for j in range(2)]
        tbuf = p0.enter_context(nc.sbuf_tensor("tbuf", [128, H], F32))
        sqbuf = p0.enter_context(nc.sbuf_tensor("sqbuf", [128, H], F32))
        rvec = p0.enter_context(nc.sbuf_tensor("rvec", [128, 1], F32))
        mvec = p0.enter_context(nc.sbuf_tensor("mvec", [128, 1], F32))
        vvec = p0.enter_context(nc.sbuf_tensor("vvec", [128, 1], F32))
        xnfb = p0.enter_context(nc.sbuf_tensor("xnfb", [128, H], F32))
        stb = [p0.enter_context(nc.sbuf_tensor(f"stb{j}", [128, H], BF16)) for j in range(2)]
        stq = [p0.enter_context(nc.sbuf_tensor(f"stq{j}", [128, H], FP8)) for j in range(2)]
        ps0 = [p0.enter_context(nc.psum_tensor(f"ps0_{j}", [128, 512], F32)) for j in range(8)]
        xld = [nc.alloc_semaphore(f"xld{j}") for j in range(2)]
        p0v = nc.alloc_semaphore("p0v")
        p0a = nc.alloc_semaphore("p0a")
        p0t = nc.alloc_semaphore("p0t")
        p0st = [nc.alloc_semaphore(f"p0st{j}") for j in range(2)]

        nc.vector.nop().then_inc(p0v, 22)
        nc.scalar.nop().then_inc(p0a, 12)
        nc.tensor.nop().then_inc(p0t, 8)

        for i0 in range(NS // 2):
            for half in range(2):
                t = i0 * 2 + half
                nc.sync.wait_ge(p0v, t * 11 + 3)
                nc.sync.dma_start(out=xbuf[half][:], in_=x_h[ts(t, 128), :]).then_inc(xld[half], 16)
                nc.vector.wait_ge(xld[half], (i0 + 1) * 16)
                nc.vector.wait_ge(p0a, 12 + t * 6)
                nc.vector.wait_ge(p0t, 8 + t * 4)
                nc.vector.reduce_sum(out=rvec[:], in_=xbuf[half][:], axis=AX.X).then_inc(p0v, 1)
                nc.vector.wait_ge(p0v, 22 + t * 11 + 1)
                nc.vector.tensor_scalar_mul(mvec[:], rvec[:], 1.0 / 2048.0).then_inc(p0v, 1)
                nc.vector.wait_ge(p0v, 22 + t * 11 + 2)
                nc.vector.tensor_scalar_sub(tbuf[:], xbuf[half][:], mvec[:]).then_inc(p0v, 1)
                nc.scalar.wait_ge(p0v, 22 + t * 11 + 3)
                nc.scalar.activation(out=sqbuf[:], in_=tbuf[:], func=AF.Square).then_inc(p0a, 1)
                nc.vector.wait_ge(p0a, 12 + t * 6 + 1)
                nc.vector.reduce_sum(out=rvec[:], in_=sqbuf[:], axis=AX.X).then_inc(p0v, 1)
                nc.vector.wait_ge(p0v, 22 + t * 11 + 4)
                nc.vector.tensor_scalar(out=vvec[:], in0=rvec[:], scalar1=1.0 / 2048.0, scalar2=1e-5,
                                        op0=ALU.mult, op1=ALU.add).then_inc(p0v, 1)
                nc.scalar.wait_ge(p0v, 22 + t * 11 + 5)
                nc.scalar.activation(out=vvec[:], in_=vvec[:], func=AF.Sqrt).then_inc(p0a, 1)
                nc.vector.wait_ge(p0a, 12 + t * 6 + 2)
                nc.vector.reciprocal(out=vvec[:], in_=vvec[:]).then_inc(p0v, 1)
                nc.vector.wait_ge(p0v, 22 + t * 11 + 6)
                nc.vector.tensor_scalar_mul(xnfb[:], tbuf[:], vvec[:]).then_inc(p0v, 1)
                nc.tensor.wait_ge(p0v, 22 + t * 11 + 7)
                nc.tensor.wait_ge(p0v, t * 11 + 11)
                nc.tensor.wait_ge(p0a, t * 6 + 6)
                for hc in range(NH):
                    bk = (t % 2) * 4 + hc // 4
                    mi = nc.tensor.transpose(ps0[bk][:, (hc % 4) * 128:(hc % 4 + 1) * 128],
                                             xnfb[:, hc * 128:(hc + 1) * 128], id32[:])
                    if hc % 4 == 3:
                        mi.then_inc(p0t, 1)
                with nc.allow_low_precision("xn casts"):
                    for j in range(4):
                        bk = (t % 2) * 4 + j
                        nc.vector.wait_ge(p0t, 8 + t * 4 + j + 1)
                        nc.vector.wait_ge(p0st[half], i0 * 32)
                        nc.vector.tensor_copy(out=stq[half][:, j * 512:(j + 1) * 512],
                                              in_=ps0[bk][:]).then_inc(p0v, 1)
                        nc.scalar.wait_ge(p0t, 8 + t * 4 + j + 1)
                        nc.scalar.wait_ge(p0st[half], i0 * 32)
                        nc.scalar.wait_ge(p0v, 22 + t * 11 + 7 + (j + 1))
                        nc.scalar.activation(out=stb[half][:, j * 512:(j + 1) * 512],
                                             in_=ps0[bk][:], func=AF.Copy).then_inc(p0a, 1)
                nc.sync.wait_ge(p0v, 22 + t * 11 + 11)
                nc.sync.wait_ge(p0a, 12 + t * 6 + 6)
                nc.sync.dma_start(out=xnbf_h[:, ts(t, 128)].rearrange("(c p) s -> p c s", p=128),
                                  in_=stb[half][:].rearrange("p (c s) -> p c s", c=NH)).then_inc(p0st[half], 16)
                nc.sync.dma_start(out=xnq8_h[:, ts(t, 128)].rearrange("(c p) s -> p c s", p=128),
                                  in_=stq[half][:].rearrange("p (c s) -> p c s", c=NH)).then_inc(p0st[half], 16)
        nc.sync.wait_ge(p0st[0], (NS // 2) * 32)
        nc.sync.wait_ge(p0st[1], (NS // 2) * 32)
        nc.all_engine_barrier()

    if upto <= 0:
        return _finish_early()
    with ExitStack() as p1:
        xnr = p1.enter_context(nc.sbuf_tensor("xnr", [128, NH * S], BF16))
        wfb = [p1.enter_context(nc.sbuf_tensor(f"wfb{j}", [128, NH * 128], F32)) for j in range(2)]
        wbb = [p1.enter_context(nc.sbuf_tensor(f"wbb{j}", [128, NH * 128], BF16)) for j in range(2)]
        bvec = [p1.enter_context(nc.sbuf_tensor(f"bvec{j}", [128, 1], F32)) for j in range(2)]
        qst = [p1.enter_context(nc.sbuf_tensor(f"qst{j}", [128, S], BF16)) for j in range(2)]
        vtst = [p1.enter_context(nc.sbuf_tensor(f"vtst{j}", [128, S], BF16)) for j in range(2)]
        xnld = nc.alloc_semaphore("xnld")
        nc.sync.dma_start(out=xnr[:].rearrange("p (c s) -> p c s", c=NH),
                          in_=xnbf_h.rearrange("(c p) s -> p c s", p=128)).then_inc(xnld, 16)
        nc.tensor.wait_ge(xnld, 16)

        def qkv_loop(kind, col0, bias0, dst, with_transpose):
            with ExitStack() as pp:
                nb1 = 4 if with_transpose else 8
                ps1 = [pp.enter_context(nc.psum_tensor(f"ps1k{kind}_{j}", [128, 512], F32))
                       for j in range(nb1)]
                psT = ([pp.enter_context(nc.psum_tensor(f"psTk{kind}_{j}", [128, 1024], BF16))
                        for j in range(4)] if with_transpose else [])
                wld = [nc.alloc_semaphore(f"wld{kind}{j}") for j in range(2)]
                bld = [nc.alloc_semaphore(f"bld{kind}{j}") for j in range(2)]
                wc = nc.alloc_semaphore(f"wc{kind}")
                mm = nc.alloc_semaphore(f"mm{kind}")
                ev = nc.alloc_semaphore(f"ev{kind}")
                st = [nc.alloc_semaphore(f"st{kind}{j}") for j in range(2)]
                tr = nc.alloc_semaphore(f"tr{kind}")
                te = nc.alloc_semaphore(f"te{kind}")
                nc.vector.nop().then_inc(wc, 2)
                nc.scalar.nop().then_inc(ev, 16)
                nc.tensor.nop().then_inc(mm, 4)
                if with_transpose:
                    nc.tensor.nop().then_inc(tr, 4)
                    nc.vector.nop().then_inc(te, 4)

                for i1 in range(8):
                    for half in range(2):
                        t = i1 * 2 + half
                        nc.sync.wait_ge(wc, t + 1)
                        nc.sync.dma_start(
                            out=wfb[half][:].rearrange("p (c m) -> p c m", c=NH),
                            in_=wqkvT_h[:, ts(col0 // 128 + t, 128)].rearrange("(c p) m -> p c m", p=128),
                        ).then_inc(wld[half], 16)
                        nc.sync.wait_ge(ev, 8 * t + 8)
                        nc.sync.dma_start(out=bvec[half][:],
                                          in_=bpm_h[ts(bias0 + t, 1), :].rearrange("a (p b) -> p (a b)", p=128)).then_inc(bld[half], 16)
                        nc.vector.wait_ge(wld[half], (i1 + 1) * 16)
                        nc.vector.wait_ge(mm, 2 * t + 2)
                        with nc.allow_low_precision("w cast"):
                            nc.vector.tensor_copy(out=wbb[half][:], in_=wfb[half][:]).then_inc(wc, 1)
                        nc.tensor.wait_ge(wc, 2 + t + 1)
                        for rnd in range(2):
                            bb0 = (rnd % 2) * 4 if nb1 == 8 else 0
                            nc.tensor.wait_ge(ev, (12 if nb1 == 8 else 16) + (t * 2 + rnd) * 4)
                            for hc in range(NH):
                                for sc in range(4):
                                    scc = rnd * 4 + sc
                                    mi = nc.tensor.matmul(
                                        ps1[bb0 + sc][:],
                                        wbb[half][:, hc * 128:(hc + 1) * 128],
                                        xnr[:, hc * S + scc * 512: hc * S + (scc + 1) * 512],
                                        start=(hc == 0), stop=(hc == NH - 1))
                            mi.then_inc(mm, 1)
                            for sc in range(4):
                                scc = rnd * 4 + sc
                                nc.scalar.wait_ge(mm, 4 + t * 2 + rnd + 1)
                                nc.scalar.wait_ge(bld[half], (i1 + 1) * 16)
                                if with_transpose:
                                    nc.scalar.wait_ge(tr, 4 + t * 4)
                                else:
                                    nc.scalar.wait_ge(st[half], i1 * 16)
                                with nc.allow_low_precision("qkv bf16"):
                                    nc.scalar.activation(out=qst[half][:, scc * 512:(scc + 1) * 512],
                                                         in_=ps1[bb0 + sc][:], func=AF.Identity,
                                                         bias=bvec[half][:]).then_inc(ev, 1)
                        if not with_transpose:
                            nc.sync.wait_ge(ev, 16 + t * 8 + 8)
                            nc.sync.dma_start(out=dst[ts(t, 128), :],
                                              in_=qst[half][:]).then_inc(st[half], 16)
                        else:
                            for st_ in range(NS):
                                bk = st_ // 8
                                nc.tensor.wait_ge(ev, 16 + t * 8 + (st_ // 16) * 4 + 4)
                                if st_ % 8 == 0:
                                    nc.tensor.wait_ge(te, t * 4 + bk + 1)
                                mi = nc.tensor.transpose(
                                    psT[bk][:, (st_ % 8) * 128:(st_ % 8 + 1) * 128],
                                    qst[half][:, st_ * 128:(st_ + 1) * 128], idbf[:])
                                if st_ % 8 == 7:
                                    mi.then_inc(tr, 1)
                            for bk in range(4):
                                nc.vector.wait_ge(tr, 4 + t * 4 + bk + 1)
                                nc.vector.wait_ge(st[half], i1 * 16)
                                nc.vector.tensor_copy(out=vtst[half][:, bk * 1024:(bk + 1) * 1024],
                                                      in_=psT[bk][:]).then_inc(te, 1)
                            nc.sync.wait_ge(te, 4 + t * 4 + 4)
                            nc.sync.dma_start(
                                out=vt_h[:, ts(t, 128)].rearrange("(st p) o -> p st o", p=128),
                                in_=vtst[half][:].rearrange("p (st o) -> p st o", st=NS),
                            ).then_inc(st[half], 16)
                nc.sync.wait_ge(st[0], 8 * 16)
                nc.sync.wait_ge(st[1], 8 * 16)
                nc.all_engine_barrier()

        qkv_loop(0, 0, 0, q_h, False)
        if upto <= 11:
            return _finish_early()
        qkv_loop(1, H, 16, k_h, False)
        if upto <= 12:
            return _finish_early()
        qkv_loop(2, 2 * H, 32, None, True)

    if upto <= 1:
        return _finish_early()
    with ExitStack() as pn:
        nb = [pn.enter_context(nc.sbuf_tensor(f"nb{j}", [128, S], BF16)) for j in range(2)]
        nsqb = pn.enter_context(nc.sbuf_tensor("nsqb", [128, S], BF16))
        qn2 = pn.enter_context(nc.sbuf_tensor("qn2", [1, S], F32))
        kn2 = pn.enter_context(nc.sbuf_tensor("kn2", [1, S], F32))
        kmaxv = pn.enter_context(nc.sbuf_tensor("kmaxv", [1, 1], F32))
        psn = [pn.enter_context(nc.psum_tensor(f"psn{j}", [128, 512], F32)) for j in range(8)]
        nld = [nc.alloc_semaphore(f"nld{j}") for j in range(2)]
        nsq = nc.alloc_semaphore("nsq")
        nmm = nc.alloc_semaphore("nmm")
        nfin = nc.alloc_semaphore("nfin")
        cqst = nc.alloc_semaphore("cqst")

        for which, srct in ((0, q_h), (1, k_h)):
            for i2 in range(8):
                for half in range(2):
                    t = i2 * 2 + half
                    gt = which * 16 + t
                    nc.sync.wait_ge(nsq, which * 16 + max(0, t - 1))
                    nc.sync.dma_start(out=nb[half][:], in_=srct[ts(t, 128), :]).then_inc(nld[half], 16)
                    nc.scalar.wait_ge(nld[half], (which * 8 + i2 + 1) * 16)
                    nc.scalar.wait_ge(nmm, gt)
                    with nc.allow_low_precision("norm sq"):
                        nc.scalar.activation(out=nsqb[:], in_=nb[half][:], func=AF.Square).then_inc(nsq, 1)
                    nc.tensor.wait_ge(nsq, gt + 1)
                    if which == 1 and t == 0:
                        nc.tensor.wait_ge(nfin, 8)
                    for sc in range(8):
                        mi = nc.tensor.matmul(psn[sc][0:1, :], onesb[:],
                                              nsqb[:, sc * 512:(sc + 1) * 512],
                                              start=(t == 0), stop=(t == 15))
                    mi.then_inc(nmm, 1)
            nc.all_engine_barrier()
            base = which * 8
            for sc in range(8):
                nc.vector.wait_ge(nfin, base + sc)
                if which == 0:
                    nc.vector.tensor_copy(out=qn2[:, sc * 512:(sc + 1) * 512],
                                          in_=psn[sc][0:1, :]).then_inc(nfin, 1)
                else:
                    nc.vector.tensor_copy(out=kn2[:, sc * 512:(sc + 1) * 512],
                                          in_=psn[sc][0:1, :]).then_inc(nfin, 1)
            nc.all_engine_barrier()
        nc.vector.wait_ge(nfin, 16)
        nc.vector.tensor_reduce(out=kmaxv[:], in_=kn2[:], axis=AX.X, op=ALU.max).then_inc(nfin, 1)
        nc.scalar.wait_ge(nfin, 17)
        nc.scalar.activation(out=qn2[:], in_=qn2[:], func=AF.Sqrt).then_inc(nfin, 1)
        nc.scalar.wait_ge(nfin, 18)
        nc.scalar.activation(out=kmaxv[:], in_=kmaxv[:], func=AF.Sqrt).then_inc(nfin, 1)
        nc.vector.wait_ge(nfin, 19)
        nc.vector.tensor_scalar(out=qn2[:], in0=qn2[:], scalar1=kmaxv[0:1, :], scalar2=CQ_MARGIN,
                                op0=ALU.mult, op1=ALU.mult).then_inc(nfin, 1)
        nc.sync.wait_ge(nfin, 20)
        nc.sync.dma_start(out=cq_h[:].rearrange("(a s) -> a s", a=1), in_=qn2[:]).then_inc(cqst, 16)
        nc.sync.wait_ge(cqst, 16)
        nc.all_engine_barrier()

    if upto <= 2:
        return _finish_early()
    with ExitStack() as p2:
        wob = p2.enter_context(nc.sbuf_tensor("wob", [128, NH * H], BF16))
        wtmp = p2.enter_context(nc.sbuf_tensor("wtmp", [128, NH * 128], F32))
        bob = p2.enter_context(nc.sbuf_tensor("bob", [128, H], F32))
        Ebuf = p2.enter_context(nc.sbuf_tensor("Ebuf", [128, NS * 512], BF16))
        qsl = p2.enter_context(nc.sbuf_tensor("qsl", [128, NH * 512], BF16))
        cbuf = [p2.enter_context(nc.sbuf_tensor(f"cbuf{j}", [128, 512], F32)) for j in range(2)]
        kbuf = [p2.enter_context(nc.sbuf_tensor(f"kbuf{j}", [128, NH * 128], BF16)) for j in range(2)]
        vbuf = [p2.enter_context(nc.sbuf_tensor(f"vbuf{j}", [128, 1024], BF16)) for j in range(2)]
        sbb = [p2.enter_context(nc.sbuf_tensor(f"sbb{j}", [128, 512], F32)) for j in range(2)]
        ctxb = p2.enter_context(nc.sbuf_tensor("ctxb", [128, NH * 512], BF16))
        sumr = p2.enter_context(nc.sbuf_tensor("sumr", [1, 512], F32))
        invc = [p2.enter_context(nc.sbuf_tensor(f"invc{j}", [128, 1], F32)) for j in range(2)]
        sats = [p2.enter_context(nc.sbuf_tensor(f"sats{j}", [128, H], F32)) for j in range(2)]
        ps2 = [p2.enter_context(nc.psum_tensor(f"ps2_{j}", [128, 512], F32)) for j in range(8)]

        wol = nc.alloc_semaphore("wol")
        wolc = nc.alloc_semaphore("wolc")
        bol = nc.alloc_semaphore("bol")
        qsld = nc.alloc_semaphore("qsld")
        cbl = [nc.alloc_semaphore(f"cbl{j}") for j in range(2)]
        kld = [nc.alloc_semaphore(f"kld{j}") for j in range(2)]
        vld = [nc.alloc_semaphore(f"vld{j}") for j in range(2)]
        smm = nc.alloc_semaphore("smm")
        ssum = nc.alloc_semaphore("ssum")
        sstt = nc.alloc_semaphore("sstt")
        sexp = nc.alloc_semaphore("sexp")
        sse = nc.alloc_semaphore("sse")
        invst = nc.alloc_semaphore("invst")
        pvm = nc.alloc_semaphore("pvm")
        pvd = nc.alloc_semaphore("pvd")
        cev = nc.alloc_semaphore("cev")
        icl = [nc.alloc_semaphore(f"icl{j}") for j in range(2)]
        omm = nc.alloc_semaphore("omm")
        ost = nc.alloc_semaphore("ost")
        sst = [nc.alloc_semaphore(f"sst{j}") for j in range(2)]

        for j in range(NH):
            nc.sync.wait_ge(wolc, j)
            nc.sync.dma_start(out=wtmp[:].rearrange("p (c m) -> p c m", c=NH),
                              in_=woT_h[:, ts(j, 128)].rearrange("(c p) m -> p c m", p=128)
                              ).then_inc(wol, 16)
            nc.vector.wait_ge(wol, (j + 1) * 16)
            with nc.allow_low_precision("w cast"):
                nc.vector.tensor_copy(
                    out=wob[:].rearrange("p (c m) -> p c m", c=NH)[:, :, j * 128:(j + 1) * 128],
                    in_=wtmp[:].rearrange("p (c m) -> p c m", c=NH)).then_inc(wolc, 1)
        nc.sync.dma_start(out=bob[:], in_=bobc_h[:, :]).then_inc(bol, 16)
        nc.vector.wait_ge(wolc, NH)
        nc.scalar.wait_ge(bol, 16)
        nc.all_engine_barrier()

        nc.scalar.nop().then_inc(ost, 8)
        nc.vector.nop().then_inc(sse, 2)

        for g in range(8):
            nc.sync.wait_ge(smm, g * 32)
            nc.sync.dma_start(out=qsl[:].rearrange("p (c s) -> p c s", c=NH),
                              in_=q_h[:, ts(g, 512)].rearrange("(c p) s -> p c s", p=128)
                              ).then_inc(qsld, 16)
            nc.sync.wait_ge(sstt, g * 32)
            nc.sync.dma_start(out=cbuf[0][:],
                              in_=cq_h[ts(g, 512)].rearrange("(a s) -> a s", a=1).broadcast_to([128, 512])
                              ).then_inc(cbl[0], 16)
            for kc in range(NS):
                par = kc % 2
                nc.sync.wait_ge(smm, g * 32 + max(0, kc - 1))
                nc.sync.dma_start(out=kbuf[par][:].rearrange("p (c m) -> p c m", c=NH),
                                  in_=k_h[:, kc * 128:(kc + 1) * 128].rearrange("(c p) m -> p c m", p=128)
                                  ).then_inc(kld[par], 16)
                nc.tensor.wait_ge(kld[par], (g * 16 + kc // 2 + 1) * 16)
                nc.tensor.wait_ge(qsld, (g + 1) * 16)
                nc.tensor.wait_ge(sstt, g * 32 + max(0, kc - 1))
                if kc <= 1:
                    nc.tensor.wait_ge(cev, g * 16)
                    nc.tensor.wait_ge(ost, 8 + g * 16)
                for hc in range(NH):
                    mi = nc.tensor.matmul(ps2[par][:], kbuf[par][:, hc * 128:(hc + 1) * 128],
                                          qsl[:, hc * 512:(hc + 1) * 512],
                                          start=(hc == 0), stop=(hc == NH - 1))
                mi.then_inc(smm, 1)
                if kc >= 1:
                    nc.tensor.wait_ge(sexp, g * 32 + kc)
                    if kc == 1:
                        nc.tensor.wait_ge(sse, g * 2 + 2)
                    mi2 = nc.tensor.matmul(ps2[2][0:1, :], onesb[:],
                                           Ebuf[:, (kc - 1) * 512:kc * 512],
                                           start=(kc == 1), stop=False)
                nc.vector.wait_ge(smm, g * 32 + kc + 1)
                nc.vector.wait_ge(cbl[0], (g + 1) * 16)
                nc.vector.wait_ge(sexp, g * 32 + max(0, kc - 1))
                nc.vector.scalar_tensor_tensor(out=sbb[par][:], in0=ps2[par][:], scalar=SCALE,
                                               in1=cbuf[0][:], op0=ALU.mult,
                                               op1=ALU.subtract).then_inc(sstt, 1)
                nc.scalar.wait_ge(sstt, g * 32 + kc + 1)
                nc.scalar.wait_ge(pvd, g)
                with nc.allow_low_precision("E bf16"):
                    nc.scalar.activation(out=Ebuf[:, kc * 512:(kc + 1) * 512], in_=sbb[par][:],
                                         func=AF.Exp).then_inc(sexp, 1)
            nc.tensor.wait_ge(sexp, g * 32 + 32)
            nc.tensor.matmul(ps2[2][0:1, :], onesb[:], Ebuf[:, (NS - 1) * 512:NS * 512],
                             start=False, stop=True).then_inc(ssum, 1)
            nc.vector.wait_ge(ssum, g + 1)
            nc.vector.tensor_copy(out=sumr[:], in_=ps2[2][0:1, :]).then_inc(sse, 1)
            nc.vector.wait_ge(sse, 2 + g * 2 + 1)
            nc.vector.reciprocal(out=sumr[:], in_=sumr[:]).then_inc(sse, 1)
            nc.sync.wait_ge(sse, 2 + g * 2 + 2)
            nc.sync.dma_start(out=inv_h[ts(g, 512)].rearrange("(a s) -> a s", a=1),
                              in_=sumr[:]).then_inc(invst, 16)
            for hh in range(2):
                for kc in range(NS):
                    par = kc % 2
                    nc.sync.wait_ge(pvm, g * 64 + hh * 32 + max(0, kc - 1))
                    nc.sync.dma_start(out=vbuf[par][:],
                                      in_=vt_h[kc * 128:(kc + 1) * 128, hh * 1024:(hh + 1) * 1024]
                                      ).then_inc(vld[par], 16)
                    nc.tensor.wait_ge(vld[par], (g * 32 + hh * 16 + kc // 2 + 1) * 16)
                    if kc == 0:
                        nc.tensor.wait_ge(sse, 2 + g * 2 + 1)
                        nc.tensor.wait_ge(sstt, g * 32 + 32)
                        nc.tensor.wait_ge(cev, g * 16 + hh * 8)
                        if hh == 1:
                            nc.tensor.wait_ge(cev, g * 16 + 8)
                    nc.tensor.wait_ge(sexp, g * 32 + kc + 1)
                    for ht in range(8):
                        mi = nc.tensor.matmul(ps2[ht][:, 0:512],
                                              vbuf[par][:, ht * 128:(ht + 1) * 128],
                                              Ebuf[:, kc * 512:(kc + 1) * 512],
                                              start=(kc == 0), stop=(kc == NS - 1))
                    mi.then_inc(pvm, 1)
                for ht in range(8):
                    nc.vector.wait_ge(pvm, g * 64 + hh * 32 + 32)
                    nc.vector.wait_ge(omm, g * 4)
                    ei = nc.vector.tensor_copy(out=ctxb[:, (hh * 8 + ht) * 512:(hh * 8 + ht + 1) * 512],
                                               in_=ps2[ht][:])
                    ei.then_inc(cev, 1)
                if hh == 1:
                    nc.vector.wait_ge(cev, g * 16 + 16)
                    nc.vector.nop().then_inc(pvd, 1)
            for qs in range(4):
                bs = (qs % 2) * 4
                nc.tensor.wait_ge(cev, g * 16 + 16)
                nc.tensor.wait_ge(ost, g * 16 + 4 * qs + 4)
                for htt in range(NH):
                    for oc in range(4):
                        mi = nc.tensor.matmul(ps2[bs + oc][:],
                                              ctxb[:, htt * 512 + qs * 128: htt * 512 + (qs + 1) * 128],
                                              wob[:, htt * H + oc * 512: htt * H + (oc + 1) * 512],
                                              start=(htt == 0), stop=(htt == NH - 1))
                mi.then_inc(omm, 1)
                nc.sync.wait_ge(invst, (g + 1) * 16)
                nc.sync.wait_ge(ost, g * 16 + 4 * qs + 4)
                nc.sync.dma_start(out=invc[qs % 2][:, 0:1],
                                  in_=inv_h[ts(g * 4 + qs, 128)].rearrange("(p a) -> p a", a=1)
                                  ).then_inc(icl[qs % 2], 16)
                nc.vector.wait_ge(omm, g * 4 + qs + 1)
                nc.vector.wait_ge(icl[qs % 2], (g * 2 + qs // 2 + 1) * 16)
                nc.vector.wait_ge(sst[qs % 2], (g * 2 + qs // 2) * 16)
                for oc in range(4):
                    nc.vector.scalar_tensor_tensor(out=sats[qs % 2][:, oc * 512:(oc + 1) * 512],
                                                   in0=ps2[bs + oc][:], scalar=invc[qs % 2][:],
                                                   in1=bob[:, oc * 512:(oc + 1) * 512],
                                                   op0=ALU.mult, op1=ALU.add).then_inc(ost, 1)
                nc.sync.wait_ge(ost, 8 + g * 16 + 4 * qs + 4)
                nc.sync.dma_start(out=sat_h[ts(g * 4 + qs, 128), :],
                                  in_=sats[qs % 2][:]).then_inc(sst[qs % 2], 16)
        nc.sync.wait_ge(sst[0], 8 * 2 * 16)
        nc.sync.wait_ge(sst[1], 8 * 2 * 16)
        nc.all_engine_barrier()

    if upto <= 3:
        return _finish_early()
    with ExitStack() as p3:
        xnq = p3.enter_context(nc.sbuf_tensor("xnq", [128, NH * S], FP8))
        wfa = [p3.enter_context(nc.sbuf_tensor(f"wfa{j}", [128, NH * 128], F32)) for j in range(2)]
        wqa = [[p3.enter_context(nc.sbuf_tensor(f"wqa{j}_{f}", [128, NH * 128], FP8))
                for f in range(2)] for j in range(2)]
        gateb = p3.enter_context(nc.sbuf_tensor("gateb", [128, S], BF16))
        ff1b = p3.enter_context(nc.sbuf_tensor("ff1b", [128, S], BF16))
        cbw = p3.enter_context(nc.sbuf_tensor("cbw", [128, S], BF16))
        actb = [p3.enter_context(nc.sbuf_tensor(f"actb{j}", [128, S], FP8)) for j in range(2)]
        ps3 = [p3.enter_context(nc.psum_tensor(f"ps3_{j}", [128, 512], F32)) for j in range(8)]
        xnqld = nc.alloc_semaphore("xnqld")
        gwl = [nc.alloc_semaphore(f"gwl{j}") for j in range(2)]
        gwc = nc.alloc_semaphore("gwc")
        gmm = nc.alloc_semaphore("gmm")
        gev = nc.alloc_semaphore("gev")
        gel = nc.alloc_semaphore("gel")
        gela = nc.alloc_semaphore("gela")
        aq8 = nc.alloc_semaphore("aq8")
        ast = [nc.alloc_semaphore(f"ast{j}") for j in range(2)]

        nc.sync.dma_start(out=xnq[:].rearrange("p (c s) -> p c s", c=NH),
                          in_=xnq8_h.rearrange("(c p) s -> p c s", p=128)).then_inc(xnqld, 16)
        nc.tensor.wait_ge(xnqld, 16)
        nc.vector.nop().then_inc(gwc, 4)
        nc.scalar.nop().then_inc(gev, 16)
        nc.vector.nop().then_inc(gel, 18)
        nc.tensor.nop().then_inc(gmm, 4)
        nc.all_engine_barrier()

        with nc.Fori(0, NHID // 2, engines=ENG) as i3:
            for half in range(2):
                t = i3 * 2 + half
                nc.sync.wait_ge(gwc, 2 * t + 2)
                nc.sync.dma_start(out=wfa[half][:].rearrange("p (c m) -> p c m", c=NH),
                                  in_=wg0_h[:, ts(t, 128)].rearrange("(c p) m -> p c m", p=128)
                                  ).then_inc(gwl[half], 16)
                nc.vector.wait_ge(gwl[half], i3 * 32 + 16)
                nc.vector.wait_ge(gmm, 2 * t + 1)
                with nc.allow_low_precision("wg cast"):
                    nc.vector.tensor_copy(out=wqa[half][0][:], in_=wfa[half][:]).then_inc(gwc, 1)
                nc.sync.wait_ge(gwc, 4 + 2 * t + 1)
                nc.sync.dma_start(out=wfa[half][:].rearrange("p (c m) -> p c m", c=NH),
                                  in_=wg1_h[:, ts(t, 128)].rearrange("(c p) m -> p c m", p=128)
                                  ).then_inc(gwl[half], 16)
                nc.vector.wait_ge(gwl[half], (i3 + 1) * 32)
                nc.vector.wait_ge(gmm, 2 * t + 2)
                with nc.allow_low_precision("wg cast"):
                    nc.vector.tensor_copy(out=wqa[half][1][:], in_=wfa[half][:]).then_inc(gwc, 1)
                for f in range(2):
                    nc.tensor.wait_ge(gwc, 4 + 2 * t + f + 1)
                    nc.tensor.wait_ge(gev, 16 * t + f * 8 + 16)
                    for hc in range(NH):
                        for sc in range(8):
                            mi = nc.tensor.matmul(
                                ps3[sc][:], wqa[half][f][:, hc * 128:(hc + 1) * 128],
                                xnq[:, hc * S + sc * 512: hc * S + (sc + 1) * 512],
                                start=(hc == 0), stop=(hc == NH - 1))
                    mi.then_inc(gmm, 1)
                    dstb = gateb if f == 0 else ff1b
                    for sc in range(8):
                        nc.scalar.wait_ge(gmm, 4 + 2 * t + f + 1)
                        nc.scalar.wait_ge(gel, 9 * t + 18)
                        with nc.allow_low_precision("ff bf16"):
                            nc.scalar.activation(out=dstb[:, sc * 512:(sc + 1) * 512],
                                                 in_=ps3[sc][:], func=AF.Copy).then_inc(gev, 1)
                def dv(n, fn):
                    nc.vector.wait_ge(gel, 18 + 9 * t + n - 1)
                    fn().then_inc(gel, 1)
                nc.vector.wait_ge(gev, 16 * t + 16 + 16)
                with nc.allow_low_precision("gelu"):
                    dv(1, lambda: nc.vector.tensor_mul(cbw[:], gateb[:], gateb[:]))
                    dv(2, lambda: nc.vector.tensor_mul(cbw[:], cbw[:], gateb[:]))
                    dv(3, lambda: nc.vector.tensor_scalar_mul(cbw[:], cbw[:], CG2))
                    dv(4, lambda: nc.vector.tensor_add(cbw[:], gateb[:], cbw[:]))
                    dv(5, lambda: nc.vector.tensor_scalar_mul(cbw[:], cbw[:], CG1))
                    nc.scalar.wait_ge(gel, 18 + 9 * t + 5)
                    nc.scalar.activation(out=cbw[:], in_=cbw[:], func=AF.Tanh).then_inc(gela, 1)
                    nc.vector.wait_ge(gela, t + 1)
                    dv(6, lambda: nc.vector.tensor_scalar_add(cbw[:], cbw[:], 1.0))
                    dv(7, lambda: nc.vector.tensor_scalar_mul(gateb[:], gateb[:], 0.5))
                    dv(8, lambda: nc.vector.tensor_mul(gateb[:], gateb[:], cbw[:]))
                    dv(9, lambda: nc.vector.tensor_mul(gateb[:], gateb[:], ff1b[:]))
                    nc.scalar.wait_ge(gel, 18 + 9 * t + 9)
                    nc.scalar.wait_ge(ast[half], i3 * 16)
                    nc.scalar.activation(out=actb[half][:], in_=gateb[:], func=AF.Copy).then_inc(aq8, 1)
                nc.sync.wait_ge(aq8, t + 1)
                nc.sync.dma_start(out=act_h[ts(t, 128), :], in_=actb[half][:]).then_inc(ast[half], 16)
        nc.sync.wait_ge(ast[0], (NHID // 2) * 16)
        nc.sync.wait_ge(ast[1], (NHID // 2) * 16)
        nc.all_engine_barrier()

    if upto <= 4:
        return _finish_early()
    with ExitStack() as p4:
        wlr = p4.enter_context(nc.sbuf_tensor("wlr", [128, NHID * H], FP8))
        wlld = nc.alloc_semaphore("wlld")
        wlc = nc.alloc_semaphore("wlc")
        with ExitStack() as p4w:
            wlf = p4w.enter_context(nc.sbuf_tensor("wlf", [128, 8 * H], F32))
            for c in range(8):
                nc.sync.wait_ge(wlc, c)
                nc.sync.dma_start(out=wlf[:].rearrange("p (c2 m) -> p c2 m", c2=8),
                                  in_=wl_h[c * 1024:(c + 1) * 1024, :].rearrange("(c2 p) m -> p c2 m", p=128)
                                  ).then_inc(wlld, 16)
                nc.vector.wait_ge(wlld, (c + 1) * 16)
                with nc.allow_low_precision("wl cast"):
                    nc.vector.tensor_copy(out=wlr[:, c * 8 * H:(c + 1) * 8 * H],
                                          in_=wlf[:]).then_inc(wlc, 1)
            nc.vector.wait_ge(wlc, 8)
            nc.all_engine_barrier()

        abuf = [p4.enter_context(nc.sbuf_tensor(f"abuf{j}", [128, NHID * 128], FP8)) for j in range(2)]
        xb2 = [p4.enter_context(nc.sbuf_tensor(f"xb2{j}", [128, H], F32)) for j in range(2)]
        tmpf = [p4.enter_context(nc.sbuf_tensor(f"tmpf{j}", [128, H], F32)) for j in range(2)]
        outsb = [p4.enter_context(nc.sbuf_tensor(f"outsb{j}", [128, H], BF16)) for j in range(2)]
        t1b = [p4.enter_context(nc.sbuf_tensor(f"t1b{j}", [128, H], BF16)) for j in range(2)]
        ps4 = [p4.enter_context(nc.psum_tensor(f"ps4_{j}", [128, 512], F32)) for j in range(8)]
        ald = [nc.alloc_semaphore(f"ald{j}") for j in range(2)]
        xld2 = [nc.alloc_semaphore(f"xld2{j}") for j in range(2)]
        sld2 = [nc.alloc_semaphore(f"sld2{j}") for j in range(2)]
        dmm = nc.alloc_semaphore("dmm")
        dev = nc.alloc_semaphore("dev")
        sil = nc.alloc_semaphore("sil")
        sila = nc.alloc_semaphore("sila")
        outst = [nc.alloc_semaphore(f"outst{j}") for j in range(2)]

        nc.tensor.nop().then_inc(dmm, 8)
        nc.scalar.nop().then_inc(dev, 8)
        nc.vector.nop().then_inc(sil, 10)
        nc.all_engine_barrier()

        for i4 in range(NS // 2):
            for half in range(2):
                t = i4 * 2 + half
                nc.sync.wait_ge(dmm, 4 * t + 4)
                nc.sync.dma_start(out=abuf[half][:].rearrange("p (c s) -> p c s", c=NHID),
                                  in_=act_h[:, ts(t, 128)].rearrange("(c p) s -> p c s", p=128)
                                  ).then_inc(ald[half], 16)
                nc.sync.wait_ge(sil, 5 * t + 5)
                nc.sync.dma_start(out=xb2[half][:], in_=x_h[ts(t, 128), :]).then_inc(xld2[half], 16)
                nc.sync.wait_ge(outst[half], i4 * 16)
                nc.sync.dma_start(out=tmpf[half][:], in_=sat_h[ts(t, 128), :]).then_inc(sld2[half], 16)
                nc.tensor.wait_ge(ald[half], (i4 + 1) * 16)
                nc.tensor.wait_ge(dev, 4 * t + 4)
                bs = (t % 2) * 4
                for c in range(NHID // 2):
                    lhs3 = abuf[half][:].rearrange("p (c2 s) -> p c2 s", c2=NHID)[:, 2 * c:2 * c + 2, :]
                    for oc in range(4):
                        rhs3 = wlr[:].rearrange("p (c2 m) -> p c2 m", c2=NHID)[:, 2 * c:2 * c + 2,
                                                                              oc * 512:(oc + 1) * 512]
                        mi = nc.tensor.matmul(ps4[bs + oc][:], lhs3, rhs3,
                                              start=(c == 0), stop=(c == NHID // 2 - 1),
                                              perf_mode=PM.DoubleRow)
                        if c == NHID // 2 - 1:
                            mi.then_inc(dmm, 1)
                for oc in range(4):
                    nc.scalar.wait_ge(dmm, 8 + 4 * t + oc + 1)
                    nc.scalar.wait_ge(sil, 5 * t + 3)
                    with nc.allow_low_precision("outs bf16"):
                        nc.scalar.activation(out=outsb[half][:, oc * 512:(oc + 1) * 512],
                                             in_=ps4[bs + oc][:], func=AF.Copy).then_inc(dev, 1)
                nc.scalar.wait_ge(dev, 8 + 4 * t + 4)
                nc.scalar.wait_ge(sil, 5 * t + 4)
                with nc.allow_low_precision("silu"):
                    nc.scalar.activation(out=t1b[half][:], in_=outsb[half][:], func=AF.Exp,
                                         scale=-1.0).then_inc(sila, 1)
                    nc.vector.wait_ge(sila, t + 1)
                    nc.vector.wait_ge(sil, 10 + 5 * t)
                    nc.vector.tensor_scalar_add(t1b[half][:], t1b[half][:], 1.0).then_inc(sil, 1)
                    nc.vector.wait_ge(sil, 10 + 5 * t + 1)
                    nc.vector.reciprocal(out=t1b[half][:], in_=t1b[half][:]).then_inc(sil, 1)
                    nc.vector.wait_ge(sil, 10 + 5 * t + 2)
                    nc.vector.tensor_mul(t1b[half][:], outsb[half][:], t1b[half][:]).then_inc(sil, 1)
                    nc.vector.wait_ge(sil, 10 + 5 * t + 3)
                    nc.vector.wait_ge(sld2[half], (i4 + 1) * 16)
                    nc.vector.tensor_mul(tmpf[half][:], tmpf[half][:], t1b[half][:]).then_inc(sil, 1)
                    nc.vector.wait_ge(sil, 10 + 5 * t + 4)
                    nc.vector.wait_ge(xld2[half], (i4 + 1) * 16)
                    nc.vector.tensor_add(tmpf[half][:], xb2[half][:], tmpf[half][:]).then_inc(sil, 1)
                nc.sync.wait_ge(sil, 10 + 5 * t + 5)
                nc.sync.dma_start(out=out_h[ts(t, 128), :], in_=tmpf[half][:]).then_inc(outst[half], 16)
        nc.sync.wait_ge(outst[0], (NS // 2) * 16)
        nc.sync.wait_ge(outst[1], (NS // 2) * 16)

    return nc


_NC_CACHE = None


def _get_nc():
    global _NC_CACHE
    if _NC_CACHE is None:
        _NC_CACHE = build_nc()
    return _NC_CACHE


def stage_inputs(L0_attn_hidden_states, L0_attn_w_qkv, L0_attn_b_qkv, L0_attn_w_out,
                 L0_attn_b_out, L0_ffn_w_gating, L0_ffn_w_linear):
    x = np.asarray(L0_attn_hidden_states)
    wqkvT = np.ascontiguousarray(np.asarray(L0_attn_w_qkv).T.astype(np.float32))
    woT = np.ascontiguousarray(np.asarray(L0_attn_w_out).T.astype(np.float32))
    wg = np.asarray(L0_ffn_w_gating)
    wl = np.ascontiguousarray(np.asarray(L0_ffn_w_linear).astype(np.float32))
    bq = np.asarray(L0_attn_b_qkv).astype(np.float32)
    bpm = np.ascontiguousarray(bq.reshape(48, 128))
    bobc = np.ascontiguousarray(np.broadcast_to(np.asarray(L0_attn_b_out).astype(np.float32), (128, H)))
    id32 = np.eye(128, dtype=np.float32)
    idbf = np.eye(128).astype(ml_dtypes.bfloat16)
    wg0 = np.ascontiguousarray(wg[0].astype(np.float32))
    wg1 = np.ascontiguousarray(wg[1].astype(np.float32))

    in_maps = []
    for b in range(x.shape[0]):
        in_maps.append({
            "x": np.ascontiguousarray(x[b], dtype=np.float32),
            "wqkvT": wqkvT, "woT": woT, "wg0": wg0, "wg1": wg1, "wl": wl,
            "bpm": bpm, "bobc": bobc, "id32": id32, "idbf": idbf,
        })
    return in_maps


def kernel(**inputs):
    in_maps = stage_inputs(**inputs)
    nc = _get_nc()
    res = run_bass_kernel_spmd(nc, in_maps, core_ids=list(range(8)))
    out = np.stack([np.asarray(r["out"]) for r in res.results]).astype(np.float32)
    return out


# version 03
# speedup vs baseline: 1.0397x; 1.0397x over previous
import numpy as np
import ml_dtypes
import concourse.bass as bass
import concourse.mybir as mybir
from concourse.bass_utils import run_bass_kernel_spmd
from contextlib import ExitStack

F32 = mybir.dt.float32
BF16 = mybir.dt.bfloat16
FP8 = mybir.dt.float8e4
AF = mybir.ActivationFunctionType
ALU = mybir.AluOpType
PM = mybir.MatmulPerfMode
E = mybir.EngineType
AX = mybir.AxisListType
ts = bass.ts

S = 4096
H = 2048
HID = 8192
NH = H // 128
NS = S // 128
NHID = HID // 128

BFnp = ml_dtypes.bfloat16
SCALE = float(np.float32(1.0) / np.float32(np.sqrt(np.float32(2048.0))))
CG1 = float(np.sqrt(2.0 / np.pi).astype(BFnp))
CG2 = float(np.asarray(0.044715, dtype=BFnp))
CQ_MARGIN = 1.02 * SCALE

ENG = [E.SP, E.PE, E.DVE, E.Activation]


def build_nc(upto=99):
    nc = bass.Bass("TRN2", target_bir_lowering=False, debug=False)

    x_h = nc.dram_tensor("x", [S, H], F32, kind="ExternalInput").ap()
    wqkvT_h = nc.dram_tensor("wqkvT", [H, 3 * H], F32, kind="ExternalInput").ap()
    woT_h = nc.dram_tensor("woT", [H, H], F32, kind="ExternalInput").ap()
    wg0_h = nc.dram_tensor("wg0", [H, HID], F32, kind="ExternalInput").ap()
    wg1_h = nc.dram_tensor("wg1", [H, HID], F32, kind="ExternalInput").ap()
    wl_h = nc.dram_tensor("wl", [HID, H], F32, kind="ExternalInput").ap()
    bpm_h = nc.dram_tensor("bpm", [48, 128], F32, kind="ExternalInput").ap()
    bobc_h = nc.dram_tensor("bobc", [128, H], F32, kind="ExternalInput").ap()
    id32_h = nc.dram_tensor("id32", [128, 128], F32, kind="ExternalInput").ap()
    idbf_h = nc.dram_tensor("idbf", [128, 128], BF16, kind="ExternalInput").ap()
    out_h = nc.dram_tensor("out", [S, H], F32, kind="ExternalOutput").ap()

    xnbf_h = nc.dram_tensor("xnbf", [H, S], BF16, kind="Internal").ap()
    xnq8_h = nc.dram_tensor("xnq8", [H, S], FP8, kind="Internal").ap()
    q_h = nc.dram_tensor("q_s", [H, S], BF16, kind="Internal").ap()
    k_h = nc.dram_tensor("k_s", [H, S], BF16, kind="Internal").ap()
    vt_h = nc.dram_tensor("vt_s", [S, H], BF16, kind="Internal").ap()
    cq_h = nc.dram_tensor("cq_s", [S], F32, kind="Internal").ap()
    inv_h = nc.dram_tensor("inv_s", [S], F32, kind="Internal").ap()
    sat_h = nc.dram_tensor("sat_s", [S, H], F32, kind="Internal").ap()
    act_h = nc.dram_tensor("act_s", [HID, S], FP8, kind="Internal").ap()

    id32 = nc.alloc_sbuf_tensor("id32s", [128, 128], F32)
    idbf = nc.alloc_sbuf_tensor("idbfs", [128, 128], BF16)
    onesb = nc.alloc_sbuf_tensor("onesb", [128, 1], BF16)
    ldc = nc.alloc_semaphore("ldc")

    nc.sync.dma_start(out=id32[:], in_=id32_h[:, :]).then_inc(ldc, 16)
    nc.sync.dma_start(out=idbf[:], in_=idbf_h[:, :]).then_inc(ldc, 16)
    nc.vector.memset(onesb[:], 1.0)
    nc.tensor.wait_ge(ldc, 32)
    nc.all_engine_barrier()

    fin_sem = nc.alloc_semaphore("fin_sem")

    def _finish_early():
        nc.sync.dma_start(out=out_h[0:128, 0:128], in_=id32[:]).then_inc(fin_sem, 16)
        nc.sync.wait_ge(fin_sem, 16)
        return nc

    with ExitStack() as p0:
        xbuf = [p0.enter_context(nc.sbuf_tensor(f"xbuf{j}", [128, H], F32)) for j in range(2)]
        tbuf = p0.enter_context(nc.sbuf_tensor("tbuf", [128, H], F32))
        sqbuf = p0.enter_context(nc.sbuf_tensor("sqbuf", [128, H], F32))
        rvec = p0.enter_context(nc.sbuf_tensor("rvec", [128, 1], F32))
        mvec = p0.enter_context(nc.sbuf_tensor("mvec", [128, 1], F32))
        vvec = p0.enter_context(nc.sbuf_tensor("vvec", [128, 1], F32))
        xnfb = p0.enter_context(nc.sbuf_tensor("xnfb", [128, H], F32))
        stb = [p0.enter_context(nc.sbuf_tensor(f"stb{j}", [128, H], BF16)) for j in range(2)]
        stq = [p0.enter_context(nc.sbuf_tensor(f"stq{j}", [128, H], FP8)) for j in range(2)]
        ps0 = [p0.enter_context(nc.psum_tensor(f"ps0_{j}", [128, 512], F32)) for j in range(8)]
        xld = [nc.alloc_semaphore(f"xld{j}") for j in range(2)]
        p0v = nc.alloc_semaphore("p0v")
        p0a = nc.alloc_semaphore("p0a")
        p0t = nc.alloc_semaphore("p0t")
        p0st = [nc.alloc_semaphore(f"p0st{j}") for j in range(2)]

        nc.vector.nop().then_inc(p0v, 22)
        nc.scalar.nop().then_inc(p0a, 12)
        nc.tensor.nop().then_inc(p0t, 8)

        for i0 in range(NS // 2):
            for half in range(2):
                t = i0 * 2 + half
                nc.sync.wait_ge(p0v, t * 11 + 3)
                nc.sync.dma_start(out=xbuf[half][:], in_=x_h[ts(t, 128), :]).then_inc(xld[half], 16)
                nc.vector.wait_ge(xld[half], (i0 + 1) * 16)
                nc.vector.wait_ge(p0a, 12 + t * 6)
                nc.vector.wait_ge(p0t, 8 + t * 4)
                nc.vector.reduce_sum(out=rvec[:], in_=xbuf[half][:], axis=AX.X).then_inc(p0v, 1)
                nc.vector.wait_ge(p0v, 22 + t * 11 + 1)
                nc.vector.tensor_scalar_mul(mvec[:], rvec[:], 1.0 / 2048.0).then_inc(p0v, 1)
                nc.vector.wait_ge(p0v, 22 + t * 11 + 2)
                nc.vector.tensor_scalar_sub(tbuf[:], xbuf[half][:], mvec[:]).then_inc(p0v, 1)
                nc.scalar.wait_ge(p0v, 22 + t * 11 + 3)
                nc.scalar.activation(out=sqbuf[:], in_=tbuf[:], func=AF.Square).then_inc(p0a, 1)
                nc.vector.wait_ge(p0a, 12 + t * 6 + 1)
                nc.vector.reduce_sum(out=rvec[:], in_=sqbuf[:], axis=AX.X).then_inc(p0v, 1)
                nc.vector.wait_ge(p0v, 22 + t * 11 + 4)
                nc.vector.tensor_scalar(out=vvec[:], in0=rvec[:], scalar1=1.0 / 2048.0, scalar2=1e-5,
                                        op0=ALU.mult, op1=ALU.add).then_inc(p0v, 1)
                nc.scalar.wait_ge(p0v, 22 + t * 11 + 5)
                nc.scalar.activation(out=vvec[:], in_=vvec[:], func=AF.Sqrt).then_inc(p0a, 1)
                nc.vector.wait_ge(p0a, 12 + t * 6 + 2)
                nc.vector.reciprocal(out=vvec[:], in_=vvec[:]).then_inc(p0v, 1)
                nc.vector.wait_ge(p0v, 22 + t * 11 + 6)
                nc.vector.tensor_scalar_mul(xnfb[:], tbuf[:], vvec[:]).then_inc(p0v, 1)
                nc.tensor.wait_ge(p0v, 22 + t * 11 + 7)
                nc.tensor.wait_ge(p0v, t * 11 + 11)
                nc.tensor.wait_ge(p0a, t * 6 + 6)
                for hc in range(NH):
                    bk = (t % 2) * 4 + hc // 4
                    mi = nc.tensor.transpose(ps0[bk][:, (hc % 4) * 128:(hc % 4 + 1) * 128],
                                             xnfb[:, hc * 128:(hc + 1) * 128], id32[:])
                    if hc % 4 == 3:
                        mi.then_inc(p0t, 1)
                with nc.allow_low_precision("xn casts"):
                    for j in range(4):
                        bk = (t % 2) * 4 + j
                        nc.vector.wait_ge(p0t, 8 + t * 4 + j + 1)
                        nc.vector.wait_ge(p0st[half], i0 * 32)
                        nc.vector.tensor_copy(out=stq[half][:, j * 512:(j + 1) * 512],
                                              in_=ps0[bk][:]).then_inc(p0v, 1)
                        nc.scalar.wait_ge(p0t, 8 + t * 4 + j + 1)
                        nc.scalar.wait_ge(p0st[half], i0 * 32)
                        nc.scalar.wait_ge(p0v, 22 + t * 11 + 7 + (j + 1))
                        nc.scalar.activation(out=stb[half][:, j * 512:(j + 1) * 512],
                                             in_=ps0[bk][:], func=AF.Copy).then_inc(p0a, 1)
                nc.sync.wait_ge(p0v, 22 + t * 11 + 11)
                nc.sync.wait_ge(p0a, 12 + t * 6 + 6)
                nc.sync.dma_start(out=xnbf_h[:, ts(t, 128)].rearrange("(c p) s -> p c s", p=128),
                                  in_=stb[half][:].rearrange("p (c s) -> p c s", c=NH)).then_inc(p0st[half], 16)
                nc.sync.dma_start(out=xnq8_h[:, ts(t, 128)].rearrange("(c p) s -> p c s", p=128),
                                  in_=stq[half][:].rearrange("p (c s) -> p c s", c=NH)).then_inc(p0st[half], 16)
        nc.sync.wait_ge(p0st[0], (NS // 2) * 32)
        nc.sync.wait_ge(p0st[1], (NS // 2) * 32)
        nc.all_engine_barrier()

    if upto <= 0:
        return _finish_early()
    with ExitStack() as p1:
        xnr = p1.enter_context(nc.sbuf_tensor("xnr", [128, NH * S], BF16))
        wfb = [p1.enter_context(nc.sbuf_tensor(f"wfb{j}", [128, NH * 128], F32)) for j in range(2)]
        wbb = [p1.enter_context(nc.sbuf_tensor(f"wbb{j}", [128, NH * 128], BF16)) for j in range(2)]
        bvec = [p1.enter_context(nc.sbuf_tensor(f"bvec{j}", [128, 1], F32)) for j in range(2)]
        qst = [p1.enter_context(nc.sbuf_tensor(f"qst{j}", [128, S], BF16)) for j in range(2)]
        vtst = [p1.enter_context(nc.sbuf_tensor(f"vtst{j}", [128, S], BF16)) for j in range(2)]
        xnld = nc.alloc_semaphore("xnld")
        nc.sync.dma_start(out=xnr[:].rearrange("p (c s) -> p c s", c=NH),
                          in_=xnbf_h.rearrange("(c p) s -> p c s", p=128)).then_inc(xnld, 16)
        nc.tensor.wait_ge(xnld, 16)

        def qkv_loop(kind, col0, bias0, dst, with_transpose):
            with ExitStack() as pp:
                nb1 = 4 if with_transpose else 8
                ps1 = [pp.enter_context(nc.psum_tensor(f"ps1k{kind}_{j}", [128, 512], F32))
                       for j in range(nb1)]
                psT = ([pp.enter_context(nc.psum_tensor(f"psTk{kind}_{j}", [128, 1024], BF16))
                        for j in range(4)] if with_transpose else [])
                wld = [nc.alloc_semaphore(f"wld{kind}{j}") for j in range(2)]
                bld = [nc.alloc_semaphore(f"bld{kind}{j}") for j in range(2)]
                wc = nc.alloc_semaphore(f"wc{kind}")
                mm = nc.alloc_semaphore(f"mm{kind}")
                ev = nc.alloc_semaphore(f"ev{kind}")
                st = [nc.alloc_semaphore(f"st{kind}{j}") for j in range(2)]
                tr = nc.alloc_semaphore(f"tr{kind}")
                te = nc.alloc_semaphore(f"te{kind}")
                nc.vector.nop().then_inc(wc, 2)
                nc.scalar.nop().then_inc(ev, 16)
                nc.tensor.nop().then_inc(mm, 4)
                if with_transpose:
                    nc.tensor.nop().then_inc(tr, 4)
                    nc.vector.nop().then_inc(te, 4)

                for i1 in range(8):
                    for half in range(2):
                        t = i1 * 2 + half
                        nc.sync.wait_ge(wc, t + 1)
                        nc.sync.dma_start(
                            out=wfb[half][:].rearrange("p (c m) -> p c m", c=NH),
                            in_=wqkvT_h[:, ts(col0 // 128 + t, 128)].rearrange("(c p) m -> p c m", p=128),
                        ).then_inc(wld[half], 16)
                        nc.sync.wait_ge(ev, 8 * t + 8)
                        nc.sync.dma_start(out=bvec[half][:],
                                          in_=bpm_h[ts(bias0 + t, 1), :].rearrange("a (p b) -> p (a b)", p=128)).then_inc(bld[half], 16)
                        nc.vector.wait_ge(wld[half], (i1 + 1) * 16)
                        nc.vector.wait_ge(mm, 2 * t + 2)
                        with nc.allow_low_precision("w cast"):
                            nc.vector.tensor_copy(out=wbb[half][:], in_=wfb[half][:]).then_inc(wc, 1)
                        nc.tensor.wait_ge(wc, 2 + t + 1)
                        for rnd in range(2):
                            bb0 = (rnd % 2) * 4 if nb1 == 8 else 0
                            nc.tensor.wait_ge(ev, (12 if nb1 == 8 else 16) + (t * 2 + rnd) * 4)
                            for hc in range(NH):
                                for sc in range(4):
                                    scc = rnd * 4 + sc
                                    mi = nc.tensor.matmul(
                                        ps1[bb0 + sc][:],
                                        wbb[half][:, hc * 128:(hc + 1) * 128],
                                        xnr[:, hc * S + scc * 512: hc * S + (scc + 1) * 512],
                                        start=(hc == 0), stop=(hc == NH - 1))
                            mi.then_inc(mm, 1)
                            for sc in range(4):
                                scc = rnd * 4 + sc
                                nc.scalar.wait_ge(mm, 4 + t * 2 + rnd + 1)
                                nc.scalar.wait_ge(bld[half], (i1 + 1) * 16)
                                if with_transpose:
                                    nc.scalar.wait_ge(tr, 4 + t * 4)
                                else:
                                    nc.scalar.wait_ge(st[half], i1 * 16)
                                with nc.allow_low_precision("qkv bf16"):
                                    nc.scalar.activation(out=qst[half][:, scc * 512:(scc + 1) * 512],
                                                         in_=ps1[bb0 + sc][:], func=AF.Identity,
                                                         bias=bvec[half][:]).then_inc(ev, 1)
                        if not with_transpose:
                            nc.sync.wait_ge(ev, 16 + t * 8 + 8)
                            nc.sync.dma_start(out=dst[ts(t, 128), :],
                                              in_=qst[half][:]).then_inc(st[half], 16)
                        else:
                            for st_ in range(NS):
                                bk = st_ // 8
                                nc.tensor.wait_ge(ev, 16 + t * 8 + (st_ // 16) * 4 + 4)
                                if st_ % 8 == 0:
                                    nc.tensor.wait_ge(te, t * 4 + bk + 1)
                                mi = nc.tensor.transpose(
                                    psT[bk][:, (st_ % 8) * 128:(st_ % 8 + 1) * 128],
                                    qst[half][:, st_ * 128:(st_ + 1) * 128], idbf[:])
                                if st_ % 8 == 7:
                                    mi.then_inc(tr, 1)
                            for bk in range(4):
                                nc.vector.wait_ge(tr, 4 + t * 4 + bk + 1)
                                nc.vector.wait_ge(st[half], i1 * 16)
                                nc.vector.tensor_copy(out=vtst[half][:, bk * 1024:(bk + 1) * 1024],
                                                      in_=psT[bk][:]).then_inc(te, 1)
                            nc.sync.wait_ge(te, 4 + t * 4 + 4)
                            nc.sync.dma_start(
                                out=vt_h[:, ts(t, 128)].rearrange("(st p) o -> p st o", p=128),
                                in_=vtst[half][:].rearrange("p (st o) -> p st o", st=NS),
                            ).then_inc(st[half], 16)
                nc.sync.wait_ge(st[0], 8 * 16)
                nc.sync.wait_ge(st[1], 8 * 16)
                nc.all_engine_barrier()

        qkv_loop(0, 0, 0, q_h, False)
        if upto <= 11:
            return _finish_early()
        qkv_loop(1, H, 16, k_h, False)
        if upto <= 12:
            return _finish_early()
        qkv_loop(2, 2 * H, 32, None, True)

    if upto <= 1:
        return _finish_early()
    with ExitStack() as pn:
        nb = [pn.enter_context(nc.sbuf_tensor(f"nb{j}", [128, S], BF16)) for j in range(2)]
        nsqb = pn.enter_context(nc.sbuf_tensor("nsqb", [128, S], BF16))
        qn2 = pn.enter_context(nc.sbuf_tensor("qn2", [1, S], F32))
        kn2 = pn.enter_context(nc.sbuf_tensor("kn2", [1, S], F32))
        kmaxv = pn.enter_context(nc.sbuf_tensor("kmaxv", [1, 1], F32))
        psn = [pn.enter_context(nc.psum_tensor(f"psn{j}", [128, 512], F32)) for j in range(8)]
        nld = [nc.alloc_semaphore(f"nld{j}") for j in range(2)]
        nsq = nc.alloc_semaphore("nsq")
        nmm = nc.alloc_semaphore("nmm")
        nfin = nc.alloc_semaphore("nfin")
        cqst = nc.alloc_semaphore("cqst")

        for which, srct in ((0, q_h), (1, k_h)):
            for i2 in range(8):
                for half in range(2):
                    t = i2 * 2 + half
                    gt = which * 16 + t
                    nc.sync.wait_ge(nsq, which * 16 + max(0, t - 1))
                    nc.sync.dma_start(out=nb[half][:], in_=srct[ts(t, 128), :]).then_inc(nld[half], 16)
                    nc.scalar.wait_ge(nld[half], (which * 8 + i2 + 1) * 16)
                    nc.scalar.wait_ge(nmm, gt)
                    with nc.allow_low_precision("norm sq"):
                        nc.scalar.activation(out=nsqb[:], in_=nb[half][:], func=AF.Square).then_inc(nsq, 1)
                    nc.tensor.wait_ge(nsq, gt + 1)
                    if which == 1 and t == 0:
                        nc.tensor.wait_ge(nfin, 8)
                    for sc in range(8):
                        mi = nc.tensor.matmul(psn[sc][0:1, :], onesb[:],
                                              nsqb[:, sc * 512:(sc + 1) * 512],
                                              start=(t == 0), stop=(t == 15))
                    mi.then_inc(nmm, 1)
            nc.all_engine_barrier()
            base = which * 8
            for sc in range(8):
                nc.vector.wait_ge(nfin, base + sc)
                if which == 0:
                    nc.vector.tensor_copy(out=qn2[:, sc * 512:(sc + 1) * 512],
                                          in_=psn[sc][0:1, :]).then_inc(nfin, 1)
                else:
                    nc.vector.tensor_copy(out=kn2[:, sc * 512:(sc + 1) * 512],
                                          in_=psn[sc][0:1, :]).then_inc(nfin, 1)
            nc.all_engine_barrier()
        nc.vector.wait_ge(nfin, 16)
        nc.vector.tensor_reduce(out=kmaxv[:], in_=kn2[:], axis=AX.X, op=ALU.max).then_inc(nfin, 1)
        nc.scalar.wait_ge(nfin, 17)
        nc.scalar.activation(out=qn2[:], in_=qn2[:], func=AF.Sqrt).then_inc(nfin, 1)
        nc.scalar.wait_ge(nfin, 18)
        nc.scalar.activation(out=kmaxv[:], in_=kmaxv[:], func=AF.Sqrt).then_inc(nfin, 1)
        nc.vector.wait_ge(nfin, 19)
        nc.vector.tensor_scalar(out=qn2[:], in0=qn2[:], scalar1=kmaxv[0:1, :], scalar2=CQ_MARGIN,
                                op0=ALU.mult, op1=ALU.mult).then_inc(nfin, 1)
        nc.sync.wait_ge(nfin, 20)
        nc.sync.dma_start(out=cq_h[:].rearrange("(a s) -> a s", a=1), in_=qn2[:]).then_inc(cqst, 16)
        nc.sync.wait_ge(cqst, 16)
        nc.all_engine_barrier()

    if upto <= 2:
        return _finish_early()
    with ExitStack() as p2:
        wob = p2.enter_context(nc.sbuf_tensor("wob", [128, NH * H], BF16))
        wtmp = p2.enter_context(nc.sbuf_tensor("wtmp", [128, NH * 128], F32))
        bob = p2.enter_context(nc.sbuf_tensor("bob", [128, H], F32))
        Ebuf = p2.enter_context(nc.sbuf_tensor("Ebuf", [128, NS * 512], BF16))
        qsl = p2.enter_context(nc.sbuf_tensor("qsl", [128, NH * 512], BF16))
        cbuf = [p2.enter_context(nc.sbuf_tensor(f"cbuf{j}", [128, 512], F32)) for j in range(2)]
        kbuf = [p2.enter_context(nc.sbuf_tensor(f"kbuf{j}", [128, NH * 128], BF16)) for j in range(2)]
        vbuf = [p2.enter_context(nc.sbuf_tensor(f"vbuf{j}", [128, 1024], BF16)) for j in range(2)]
        sbb = [p2.enter_context(nc.sbuf_tensor(f"sbb{j}", [128, 512], F32)) for j in range(2)]
        ctxb = p2.enter_context(nc.sbuf_tensor("ctxb", [128, NH * 512], BF16))
        sumr = p2.enter_context(nc.sbuf_tensor("sumr", [1, 512], F32))
        invc = [p2.enter_context(nc.sbuf_tensor(f"invc{j}", [128, 1], F32)) for j in range(2)]
        sats = [p2.enter_context(nc.sbuf_tensor(f"sats{j}", [128, H], F32)) for j in range(2)]
        ps2 = [p2.enter_context(nc.psum_tensor(f"ps2_{j}", [128, 512], F32)) for j in range(8)]

        wol = nc.alloc_semaphore("wol")
        wolc = nc.alloc_semaphore("wolc")
        bol = nc.alloc_semaphore("bol")
        qsld = nc.alloc_semaphore("qsld")
        cbl = [nc.alloc_semaphore(f"cbl{j}") for j in range(2)]
        kld = [nc.alloc_semaphore(f"kld{j}") for j in range(2)]
        vld = [nc.alloc_semaphore(f"vld{j}") for j in range(2)]
        smm = nc.alloc_semaphore("smm")
        ssum = nc.alloc_semaphore("ssum")
        sstt = nc.alloc_semaphore("sstt")
        sexp = nc.alloc_semaphore("sexp")
        sse = nc.alloc_semaphore("sse")
        invst = nc.alloc_semaphore("invst")
        pvm = nc.alloc_semaphore("pvm")
        pvd = nc.alloc_semaphore("pvd")
        cev = nc.alloc_semaphore("cev")
        icl = [nc.alloc_semaphore(f"icl{j}") for j in range(2)]
        omm = nc.alloc_semaphore("omm")
        ost = nc.alloc_semaphore("ost")
        sst = [nc.alloc_semaphore(f"sst{j}") for j in range(2)]

        for j in range(NH):
            nc.sync.wait_ge(wolc, j)
            nc.sync.dma_start(out=wtmp[:].rearrange("p (c m) -> p c m", c=NH),
                              in_=woT_h[:, ts(j, 128)].rearrange("(c p) m -> p c m", p=128)
                              ).then_inc(wol, 16)
            nc.vector.wait_ge(wol, (j + 1) * 16)
            with nc.allow_low_precision("w cast"):
                nc.vector.tensor_copy(
                    out=wob[:].rearrange("p (c m) -> p c m", c=NH)[:, :, j * 128:(j + 1) * 128],
                    in_=wtmp[:].rearrange("p (c m) -> p c m", c=NH)).then_inc(wolc, 1)
        nc.sync.dma_start(out=bob[:], in_=bobc_h[:, :]).then_inc(bol, 16)
        nc.vector.wait_ge(wolc, NH)
        nc.scalar.wait_ge(bol, 16)
        nc.all_engine_barrier()

        nc.scalar.nop().then_inc(ost, 8)
        nc.vector.nop().then_inc(sse, 2)

        for g in range(8):
            nc.sync.wait_ge(smm, g * 32)
            nc.sync.dma_start(out=qsl[:].rearrange("p (c s) -> p c s", c=NH),
                              in_=q_h[:, ts(g, 512)].rearrange("(c p) s -> p c s", p=128)
                              ).then_inc(qsld, 16)
            nc.sync.wait_ge(sstt, g * 32)
            nc.sync.dma_start(out=cbuf[0][:],
                              in_=cq_h[ts(g, 512)].rearrange("(a s) -> a s", a=1).broadcast_to([128, 512])
                              ).then_inc(cbl[0], 16)
            for kc in range(NS):
                par = kc % 2
                nc.sync.wait_ge(smm, g * 32 + max(0, kc - 1))
                nc.sync.dma_start(out=kbuf[par][:].rearrange("p (c m) -> p c m", c=NH),
                                  in_=k_h[:, kc * 128:(kc + 1) * 128].rearrange("(c p) m -> p c m", p=128)
                                  ).then_inc(kld[par], 16)
                nc.tensor.wait_ge(kld[par], (g * 16 + kc // 2 + 1) * 16)
                nc.tensor.wait_ge(qsld, (g + 1) * 16)
                nc.tensor.wait_ge(sstt, g * 32 + max(0, kc - 1))
                if kc <= 1:
                    nc.tensor.wait_ge(cev, g * 16)
                    nc.tensor.wait_ge(ost, 8 + g * 16)
                for hc in range(NH):
                    mi = nc.tensor.matmul(ps2[par][:], kbuf[par][:, hc * 128:(hc + 1) * 128],
                                          qsl[:, hc * 512:(hc + 1) * 512],
                                          start=(hc == 0), stop=(hc == NH - 1))
                mi.then_inc(smm, 1)
                if kc >= 1:
                    nc.tensor.wait_ge(sexp, g * 32 + kc)
                    if kc == 1:
                        nc.tensor.wait_ge(sse, g * 2 + 2)
                    mi2 = nc.tensor.matmul(ps2[2][0:1, :], onesb[:],
                                           Ebuf[:, (kc - 1) * 512:kc * 512],
                                           start=(kc == 1), stop=False)
                nc.vector.wait_ge(smm, g * 32 + kc + 1)
                nc.vector.wait_ge(cbl[0], (g + 1) * 16)
                nc.vector.wait_ge(sexp, g * 32 + max(0, kc - 1))
                nc.vector.scalar_tensor_tensor(out=sbb[par][:], in0=ps2[par][:], scalar=SCALE,
                                               in1=cbuf[0][:], op0=ALU.mult,
                                               op1=ALU.subtract).then_inc(sstt, 1)
                nc.scalar.wait_ge(sstt, g * 32 + kc + 1)
                nc.scalar.wait_ge(pvd, g)
                with nc.allow_low_precision("E bf16"):
                    nc.scalar.activation(out=Ebuf[:, kc * 512:(kc + 1) * 512], in_=sbb[par][:],
                                         func=AF.Exp).then_inc(sexp, 1)
            nc.tensor.wait_ge(sexp, g * 32 + 32)
            nc.tensor.matmul(ps2[2][0:1, :], onesb[:], Ebuf[:, (NS - 1) * 512:NS * 512],
                             start=False, stop=True).then_inc(ssum, 1)
            nc.vector.wait_ge(ssum, g + 1)
            nc.vector.tensor_copy(out=sumr[:], in_=ps2[2][0:1, :]).then_inc(sse, 1)
            nc.vector.wait_ge(sse, 2 + g * 2 + 1)
            nc.vector.reciprocal(out=sumr[:], in_=sumr[:]).then_inc(sse, 1)
            nc.sync.wait_ge(sse, 2 + g * 2 + 2)
            nc.sync.dma_start(out=inv_h[ts(g, 512)].rearrange("(a s) -> a s", a=1),
                              in_=sumr[:]).then_inc(invst, 16)
            for hh in range(2):
                for kc in range(NS):
                    par = kc % 2
                    nc.sync.wait_ge(pvm, g * 64 + hh * 32 + max(0, kc - 1))
                    nc.sync.dma_start(out=vbuf[par][:],
                                      in_=vt_h[kc * 128:(kc + 1) * 128, hh * 1024:(hh + 1) * 1024]
                                      ).then_inc(vld[par], 16)
                    nc.tensor.wait_ge(vld[par], (g * 32 + hh * 16 + kc // 2 + 1) * 16)
                    if kc == 0:
                        nc.tensor.wait_ge(sse, 2 + g * 2 + 1)
                        nc.tensor.wait_ge(sstt, g * 32 + 32)
                        nc.tensor.wait_ge(cev, g * 16 + hh * 8)
                        if hh == 1:
                            nc.tensor.wait_ge(cev, g * 16 + 8)
                    nc.tensor.wait_ge(sexp, g * 32 + kc + 1)
                    for ht in range(8):
                        mi = nc.tensor.matmul(ps2[ht][:, 0:512],
                                              vbuf[par][:, ht * 128:(ht + 1) * 128],
                                              Ebuf[:, kc * 512:(kc + 1) * 512],
                                              start=(kc == 0), stop=(kc == NS - 1))
                    mi.then_inc(pvm, 1)
                for ht in range(8):
                    nc.vector.wait_ge(pvm, g * 64 + hh * 32 + 32)
                    nc.vector.wait_ge(omm, g * 4)
                    ei = nc.vector.tensor_copy(out=ctxb[:, (hh * 8 + ht) * 512:(hh * 8 + ht + 1) * 512],
                                               in_=ps2[ht][:])
                    ei.then_inc(cev, 1)
                if hh == 1:
                    nc.vector.wait_ge(cev, g * 16 + 16)
                    nc.vector.nop().then_inc(pvd, 1)
            for qs in range(4):
                bs = (qs % 2) * 4
                nc.tensor.wait_ge(cev, g * 16 + 16)
                nc.tensor.wait_ge(ost, g * 16 + 4 * qs + 4)
                for htt in range(NH):
                    for oc in range(4):
                        mi = nc.tensor.matmul(ps2[bs + oc][:],
                                              ctxb[:, htt * 512 + qs * 128: htt * 512 + (qs + 1) * 128],
                                              wob[:, htt * H + oc * 512: htt * H + (oc + 1) * 512],
                                              start=(htt == 0), stop=(htt == NH - 1))
                mi.then_inc(omm, 1)
                nc.sync.wait_ge(invst, (g + 1) * 16)
                nc.sync.wait_ge(ost, g * 16 + 4 * qs + 4)
                nc.sync.dma_start(out=invc[qs % 2][:, 0:1],
                                  in_=inv_h[ts(g * 4 + qs, 128)].rearrange("(p a) -> p a", a=1)
                                  ).then_inc(icl[qs % 2], 16)
                nc.vector.wait_ge(omm, g * 4 + qs + 1)
                nc.vector.wait_ge(icl[qs % 2], (g * 2 + qs // 2 + 1) * 16)
                nc.vector.wait_ge(sst[qs % 2], (g * 2 + qs // 2) * 16)
                for oc in range(4):
                    nc.vector.scalar_tensor_tensor(out=sats[qs % 2][:, oc * 512:(oc + 1) * 512],
                                                   in0=ps2[bs + oc][:], scalar=invc[qs % 2][:],
                                                   in1=bob[:, oc * 512:(oc + 1) * 512],
                                                   op0=ALU.mult, op1=ALU.add).then_inc(ost, 1)
                nc.sync.wait_ge(ost, 8 + g * 16 + 4 * qs + 4)
                nc.sync.dma_start(out=sat_h[ts(g * 4 + qs, 128), :],
                                  in_=sats[qs % 2][:]).then_inc(sst[qs % 2], 16)
        nc.sync.wait_ge(sst[0], 8 * 2 * 16)
        nc.sync.wait_ge(sst[1], 8 * 2 * 16)
        nc.all_engine_barrier()

    if upto <= 3:
        return _finish_early()
    with ExitStack() as p3:
        xnq = p3.enter_context(nc.sbuf_tensor("xnq", [128, NH * S], FP8))
        wfa = [p3.enter_context(nc.sbuf_tensor(f"wfa{j}", [128, NH * 128], F32)) for j in range(2)]
        wqa = [[p3.enter_context(nc.sbuf_tensor(f"wqa{j}_{f}", [128, NH * 128], FP8))
                for f in range(2)] for j in range(2)]
        gateb = [p3.enter_context(nc.sbuf_tensor(f"gateb{j}", [128, S], BF16)) for j in range(2)]
        ff1b = [p3.enter_context(nc.sbuf_tensor(f"ff1b{j}", [128, S], BF16)) for j in range(2)]
        cbw = p3.enter_context(nc.sbuf_tensor("cbw", [128, S], BF16))
        actb = [p3.enter_context(nc.sbuf_tensor(f"actb{j}", [128, S], FP8)) for j in range(2)]
        ps3 = [p3.enter_context(nc.psum_tensor(f"ps3_{j}", [128, 512], F32)) for j in range(8)]
        xnqld = nc.alloc_semaphore("xnqld")
        gwl = [nc.alloc_semaphore(f"gwl{j}") for j in range(2)]
        gwc = nc.alloc_semaphore("gwc")
        gmm = nc.alloc_semaphore("gmm")
        gev = nc.alloc_semaphore("gev")
        gel = nc.alloc_semaphore("gel")
        gela = nc.alloc_semaphore("gela")
        aq8 = nc.alloc_semaphore("aq8")
        ast = [nc.alloc_semaphore(f"ast{j}") for j in range(2)]

        nc.sync.dma_start(out=xnq[:].rearrange("p (c s) -> p c s", c=NH),
                          in_=xnq8_h.rearrange("(c p) s -> p c s", p=128)).then_inc(xnqld, 16)
        nc.tensor.wait_ge(xnqld, 16)
        nc.vector.nop().then_inc(gwc, 4)
        nc.scalar.nop().then_inc(gev, 16)
        nc.vector.nop().then_inc(gel, 18)
        nc.tensor.nop().then_inc(gmm, 8)
        nc.scalar.nop().then_inc(aq8, 2)
        nc.all_engine_barrier()

        with nc.Fori(0, NHID // 2, engines=ENG) as i3:
            for half in range(2):
                t = i3 * 2 + half
                nc.sync.wait_ge(gwc, 2 * t + 2)
                nc.sync.dma_start(out=wfa[half][:].rearrange("p (c m) -> p c m", c=NH),
                                  in_=wg0_h[:, ts(t, 128)].rearrange("(c p) m -> p c m", p=128)
                                  ).then_inc(gwl[half], 16)
                nc.vector.wait_ge(gwl[half], i3 * 32 + 16)
                nc.vector.wait_ge(gmm, 4 * t + 3)
                with nc.allow_low_precision("wg cast"):
                    nc.vector.tensor_copy(out=wqa[half][0][:], in_=wfa[half][:]).then_inc(gwc, 1)
                nc.sync.wait_ge(gwc, 4 + 2 * t + 1)
                nc.sync.dma_start(out=wfa[half][:].rearrange("p (c m) -> p c m", c=NH),
                                  in_=wg1_h[:, ts(t, 128)].rearrange("(c p) m -> p c m", p=128)
                                  ).then_inc(gwl[half], 16)
                nc.vector.wait_ge(gwl[half], (i3 + 1) * 32)
                nc.vector.wait_ge(gmm, 4 * t + 4)
                with nc.allow_low_precision("wg cast"):
                    nc.vector.tensor_copy(out=wqa[half][1][:], in_=wfa[half][:]).then_inc(gwc, 1)
                for r in range(4):
                    f = r % 2
                    scb = (r // 2) * 4
                    bb = f * 4
                    R = 4 * t + r
                    nc.tensor.wait_ge(gwc, 4 + 2 * t + f + 1)
                    nc.tensor.wait_ge(gev, 4 * R + 12)
                    for hc in range(NH):
                        for sc in range(4):
                            mi = nc.tensor.matmul(
                                ps3[bb + sc][:], wqa[half][f][:, hc * 128:(hc + 1) * 128],
                                xnq[:, hc * S + (scb + sc) * 512: hc * S + (scb + sc + 1) * 512],
                                start=(hc == 0), stop=(hc == NH - 1))
                    mi.then_inc(gmm, 1)
                    dstb = gateb[t % 2] if f == 0 else ff1b[t % 2]
                    for sc in range(4):
                        nc.scalar.wait_ge(gmm, 8 + R + 1)
                        nc.scalar.wait_ge(gel, 9 * t + 9)
                        nc.scalar.wait_ge(aq8, t + 1)
                        with nc.allow_low_precision("ff bf16"):
                            nc.scalar.activation(out=dstb[:, (scb + sc) * 512:(scb + sc + 1) * 512],
                                                 in_=ps3[bb + sc][:], func=AF.Copy).then_inc(gev, 1)
                def dv(n, fn):
                    nc.vector.wait_ge(gel, 18 + 9 * t + n - 1)
                    fn().then_inc(gel, 1)
                nc.vector.wait_ge(gev, 16 * t + 16 + 16)
                gb = gateb[t % 2]
                fb = ff1b[t % 2]
                with nc.allow_low_precision("gelu"):
                    dv(1, lambda: nc.vector.tensor_mul(cbw[:], gb[:], gb[:]))
                    dv(2, lambda: nc.vector.tensor_mul(cbw[:], cbw[:], gb[:]))
                    dv(3, lambda: nc.vector.tensor_scalar_mul(cbw[:], cbw[:], CG2))
                    dv(4, lambda: nc.vector.tensor_add(cbw[:], gb[:], cbw[:]))
                    dv(5, lambda: nc.vector.tensor_scalar_mul(cbw[:], cbw[:], CG1))
                    nc.scalar.wait_ge(gel, 18 + 9 * t + 5)
                    nc.scalar.activation(out=cbw[:], in_=cbw[:], func=AF.Tanh).then_inc(gela, 1)
                    nc.vector.wait_ge(gela, t + 1)
                    dv(6, lambda: nc.vector.tensor_scalar_add(cbw[:], cbw[:], 1.0))
                    dv(7, lambda: nc.vector.tensor_scalar_mul(gb[:], gb[:], 0.5))
                    dv(8, lambda: nc.vector.tensor_mul(gb[:], gb[:], cbw[:]))
                    dv(9, lambda: nc.vector.tensor_mul(gb[:], gb[:], fb[:]))
                    nc.scalar.wait_ge(gel, 18 + 9 * t + 9)
                    nc.scalar.wait_ge(ast[half], i3 * 16)
                    nc.scalar.activation(out=actb[half][:], in_=gb[:], func=AF.Copy).then_inc(aq8, 1)
                nc.sync.wait_ge(aq8, 2 + t + 1)
                nc.sync.dma_start(out=act_h[ts(t, 128), :], in_=actb[half][:]).then_inc(ast[half], 16)
        nc.sync.wait_ge(ast[0], (NHID // 2) * 16)
        nc.sync.wait_ge(ast[1], (NHID // 2) * 16)
        nc.all_engine_barrier()

    if upto <= 4:
        return _finish_early()
    with ExitStack() as p4:
        wlr = p4.enter_context(nc.sbuf_tensor("wlr", [128, NHID * H], FP8))
        wlld = nc.alloc_semaphore("wlld")
        wlc = nc.alloc_semaphore("wlc")
        with ExitStack() as p4w:
            wlf = p4w.enter_context(nc.sbuf_tensor("wlf", [128, 8 * H], F32))
            for c in range(8):
                nc.sync.wait_ge(wlc, c)
                nc.sync.dma_start(out=wlf[:].rearrange("p (c2 m) -> p c2 m", c2=8),
                                  in_=wl_h[c * 1024:(c + 1) * 1024, :].rearrange("(c2 p) m -> p c2 m", p=128)
                                  ).then_inc(wlld, 16)
                nc.vector.wait_ge(wlld, (c + 1) * 16)
                with nc.allow_low_precision("wl cast"):
                    nc.vector.tensor_copy(out=wlr[:, c * 8 * H:(c + 1) * 8 * H],
                                          in_=wlf[:]).then_inc(wlc, 1)
            nc.vector.wait_ge(wlc, 8)
            nc.all_engine_barrier()

        abuf = [p4.enter_context(nc.sbuf_tensor(f"abuf{j}", [128, NHID * 128], FP8)) for j in range(2)]
        xb2 = [p4.enter_context(nc.sbuf_tensor(f"xb2{j}", [128, H], F32)) for j in range(2)]
        tmpf = [p4.enter_context(nc.sbuf_tensor(f"tmpf{j}", [128, H], F32)) for j in range(2)]
        outsb = [p4.enter_context(nc.sbuf_tensor(f"outsb{j}", [128, H], BF16)) for j in range(2)]
        t1b = [p4.enter_context(nc.sbuf_tensor(f"t1b{j}", [128, H], BF16)) for j in range(2)]
        ps4 = [p4.enter_context(nc.psum_tensor(f"ps4_{j}", [128, 512], F32)) for j in range(8)]
        ald = [nc.alloc_semaphore(f"ald{j}") for j in range(2)]
        xld2 = [nc.alloc_semaphore(f"xld2{j}") for j in range(2)]
        sld2 = [nc.alloc_semaphore(f"sld2{j}") for j in range(2)]
        dmm = nc.alloc_semaphore("dmm")
        dev = nc.alloc_semaphore("dev")
        sil = nc.alloc_semaphore("sil")
        sila = nc.alloc_semaphore("sila")
        outst = [nc.alloc_semaphore(f"outst{j}") for j in range(2)]

        nc.tensor.nop().then_inc(dmm, 8)
        nc.scalar.nop().then_inc(dev, 8)
        nc.vector.nop().then_inc(sil, 10)
        nc.all_engine_barrier()

        for i4 in range(NS // 2):
            for half in range(2):
                t = i4 * 2 + half
                nc.sync.wait_ge(dmm, 4 * t + 4)
                nc.sync.dma_start(out=abuf[half][:].rearrange("p (c s) -> p c s", c=NHID),
                                  in_=act_h[:, ts(t, 128)].rearrange("(c p) s -> p c s", p=128)
                                  ).then_inc(ald[half], 16)
                nc.sync.wait_ge(sil, 5 * t + 5)
                nc.sync.dma_start(out=xb2[half][:], in_=x_h[ts(t, 128), :]).then_inc(xld2[half], 16)
                nc.sync.wait_ge(outst[half], i4 * 16)
                nc.sync.dma_start(out=tmpf[half][:], in_=sat_h[ts(t, 128), :]).then_inc(sld2[half], 16)
                nc.tensor.wait_ge(ald[half], (i4 + 1) * 16)
                nc.tensor.wait_ge(dev, 4 * t + 4)
                bs = (t % 2) * 4
                for c in range(NHID // 2):
                    lhs3 = abuf[half][:].rearrange("p (c2 s) -> p c2 s", c2=NHID)[:, 2 * c:2 * c + 2, :]
                    for oc in range(4):
                        rhs3 = wlr[:].rearrange("p (c2 m) -> p c2 m", c2=NHID)[:, 2 * c:2 * c + 2,
                                                                              oc * 512:(oc + 1) * 512]
                        mi = nc.tensor.matmul(ps4[bs + oc][:], lhs3, rhs3,
                                              start=(c == 0), stop=(c == NHID // 2 - 1),
                                              perf_mode=PM.DoubleRow)
                        if c == NHID // 2 - 1:
                            mi.then_inc(dmm, 1)
                for oc in range(4):
                    nc.scalar.wait_ge(dmm, 8 + 4 * t + oc + 1)
                    nc.scalar.wait_ge(sil, 5 * t + 3)
                    with nc.allow_low_precision("outs bf16"):
                        nc.scalar.activation(out=outsb[half][:, oc * 512:(oc + 1) * 512],
                                             in_=ps4[bs + oc][:], func=AF.Copy).then_inc(dev, 1)
                nc.scalar.wait_ge(dev, 8 + 4 * t + 4)
                nc.scalar.wait_ge(sil, 5 * t + 4)
                with nc.allow_low_precision("silu"):
                    nc.scalar.activation(out=t1b[half][:], in_=outsb[half][:], func=AF.Exp,
                                         scale=-1.0).then_inc(sila, 1)
                    nc.vector.wait_ge(sila, t + 1)
                    nc.vector.wait_ge(sil, 10 + 5 * t)
                    nc.vector.tensor_scalar_add(t1b[half][:], t1b[half][:], 1.0).then_inc(sil, 1)
                    nc.vector.wait_ge(sil, 10 + 5 * t + 1)
                    nc.vector.reciprocal(out=t1b[half][:], in_=t1b[half][:]).then_inc(sil, 1)
                    nc.vector.wait_ge(sil, 10 + 5 * t + 2)
                    nc.vector.tensor_mul(t1b[half][:], outsb[half][:], t1b[half][:]).then_inc(sil, 1)
                    nc.vector.wait_ge(sil, 10 + 5 * t + 3)
                    nc.vector.wait_ge(sld2[half], (i4 + 1) * 16)
                    nc.vector.tensor_mul(tmpf[half][:], tmpf[half][:], t1b[half][:]).then_inc(sil, 1)
                    nc.vector.wait_ge(sil, 10 + 5 * t + 4)
                    nc.vector.wait_ge(xld2[half], (i4 + 1) * 16)
                    nc.vector.tensor_add(tmpf[half][:], xb2[half][:], tmpf[half][:]).then_inc(sil, 1)
                nc.sync.wait_ge(sil, 10 + 5 * t + 5)
                nc.sync.dma_start(out=out_h[ts(t, 128), :], in_=tmpf[half][:]).then_inc(outst[half], 16)
        nc.sync.wait_ge(outst[0], (NS // 2) * 16)
        nc.sync.wait_ge(outst[1], (NS // 2) * 16)

    return nc


_NC_CACHE = None


def _get_nc():
    global _NC_CACHE
    if _NC_CACHE is None:
        _NC_CACHE = build_nc()
    return _NC_CACHE


def stage_inputs(L0_attn_hidden_states, L0_attn_w_qkv, L0_attn_b_qkv, L0_attn_w_out,
                 L0_attn_b_out, L0_ffn_w_gating, L0_ffn_w_linear):
    x = np.asarray(L0_attn_hidden_states)
    wqkvT = np.ascontiguousarray(np.asarray(L0_attn_w_qkv).T.astype(np.float32))
    woT = np.ascontiguousarray(np.asarray(L0_attn_w_out).T.astype(np.float32))
    wg = np.asarray(L0_ffn_w_gating)
    wl = np.ascontiguousarray(np.asarray(L0_ffn_w_linear).astype(np.float32))
    bq = np.asarray(L0_attn_b_qkv).astype(np.float32)
    bpm = np.ascontiguousarray(bq.reshape(48, 128))
    bobc = np.ascontiguousarray(np.broadcast_to(np.asarray(L0_attn_b_out).astype(np.float32), (128, H)))
    id32 = np.eye(128, dtype=np.float32)
    idbf = np.eye(128).astype(ml_dtypes.bfloat16)
    wg0 = np.ascontiguousarray(wg[0].astype(np.float32))
    wg1 = np.ascontiguousarray(wg[1].astype(np.float32))

    in_maps = []
    for b in range(x.shape[0]):
        in_maps.append({
            "x": np.ascontiguousarray(x[b], dtype=np.float32),
            "wqkvT": wqkvT, "woT": woT, "wg0": wg0, "wg1": wg1, "wl": wl,
            "bpm": bpm, "bobc": bobc, "id32": id32, "idbf": idbf,
        })
    return in_maps


def kernel(**inputs):
    in_maps = stage_inputs(**inputs)
    nc = _get_nc()
    res = run_bass_kernel_spmd(nc, in_maps, core_ids=list(range(8)))
    out = np.stack([np.asarray(r["out"]) for r in res.results]).astype(np.float32)
    return out
